# version 07 (final)
# baseline (speedup 1.0000x reference)
from contextlib import ExitStack

import numpy as np
import ml_dtypes

import concourse.bass as bass
import concourse.mybir as mybir
from concourse.bass_utils import run_bass_kernel_spmd
from concourse.alu_op_type import AluOpType as OP

F32 = mybir.dt.float32
BF16 = mybir.dt.bfloat16
AF = mybir.ActivationFunctionType
AX = mybir.AxisListType

B, T, D = 16, 2048, 1024
NCORE = 8
SPC = B // NCORE
H_OUT, H_IN, DH = 16, 8, 64
WIN = 512
HID, HIDP = 2730, 2816
NT = T // 512
NC = D // 128

SWAPMASK = []
for _i in range(16):
    SWAPMASK += [2 * _i + 1, 2 * _i]

DEBUG_TAPS = False
DEBUG_LIGHT = False
PARANOID = True
FIXED_M = False
FIXED_G = False
FIXED_G2 = False
EXPSHIFT = 70.0



def _host_prep(inp):
    f32 = np.float32
    bf16 = ml_dtypes.bfloat16
    d = {}
    x = np.asarray(inp["L0_attn_x"], f32)
    d["_x_per_core"] = [np.ascontiguousarray(
        x[2 * i:2 * i + 2].transpose(0, 2, 1)) for i in range(NCORE)]

    d["wqkvT"] = np.ascontiguousarray(np.asarray(inp["L0_attn_W_qkv"], f32).T)
    d["woutT"] = np.ascontiguousarray(np.asarray(inp["L0_attn_W_out"], f32).T)

    perm = np.empty(64, np.int64)
    perm[0::2] = np.arange(32)
    perm[1::2] = np.arange(32) + 32

    iwqkv = np.array(np.asarray(inp["L0_ffn_wqkv"], f32))
    for h in range(H_IN):
        for base in (0, 512):
            iwqkv[:, :, base + h * 64 + np.arange(64)] = \
                iwqkv[:, :, base + h * 64 + perm]
    d["iwqkv"] = np.ascontiguousarray(iwqkv)
    d["iwo"] = np.ascontiguousarray(np.asarray(inp["L0_ffn_wo"], f32))

    w1 = np.asarray(inp["L0_ffn_w1"], f32)
    w1p = np.zeros((2, D, 2 * HIDP), f32)
    w1p[:, :, :HID] = w1[:, :, :HID]
    w1p[:, :, HIDP:HIDP + HID] = w1[:, :, HID:]
    d["w1p"] = w1p
    w2 = np.asarray(inp["L0_ffn_w2"], f32)
    w2p = np.zeros((2, HIDP, D), f32)
    w2p[:, :HID] = w2
    d["w2p"] = np.ascontiguousarray(w2p)

    b1 = np.asarray(inp["L0_ffn_b1"], f32)
    b1p = np.zeros((2, 2 * HIDP), f32)
    b1p[:, :HID] = b1[:, :HID]
    b1p[:, HIDP:HIDP + HID] = b1[:, HID:]
    d["b1t"] = np.ascontiguousarray(
        b1p.reshape(2, 44, 128).transpose(2, 0, 1).reshape(128, 88))
    d["b2t"] = np.ascontiguousarray(
        np.asarray(inp["L0_ffn_b2"], f32).reshape(2, 8, 128)
        .transpose(2, 0, 1).reshape(128, 16))
    d["lngt"] = np.ascontiguousarray(
        np.asarray(inp["L0_ffn_ln_g"], f32).reshape(2, 8, 128)
        .transpose(2, 0, 1).reshape(128, 16))
    d["lnbt"] = np.ascontiguousarray(
        np.asarray(inp["L0_ffn_ln_b"], f32).reshape(2, 8, 128)
        .transpose(2, 0, 1).reshape(128, 16))

    rows = perm[np.arange(128) % 64]
    rows_sw = perm[(np.arange(128) % 64) ^ 1]
    d["qst"] = np.ascontiguousarray(np.asarray(inp["L0_ffn_q_scale"], f32)[:, rows].T)
    d["kst"] = np.ascontiguousarray(np.asarray(inp["L0_ffn_k_scale"], f32)[:, rows].T)
    d["qswt"] = np.ascontiguousarray(
        np.asarray(inp["L0_ffn_q_scale"], f32)[:, rows_sw].T)
    d["kswt"] = np.ascontiguousarray(
        np.asarray(inp["L0_ffn_k_scale"], f32)[:, rows_sw].T)

    half = np.arange(0, 64, 2, dtype=np.float64)
    inv_freq = 1.0 / 10000.0 ** (half / 64.0)
    t = np.arange(T, dtype=np.float64)
    fr = t[:, None] * inv_freq[None, :]
    base_ = (half + 0.4 * 64.0) / (1.4 * 64.0)
    power_q = (t % WIN) / float(WIN // 2)
    sq_rel = base_[None, :] ** power_q[:, None]
    cos, sin = np.cos(fr), np.sin(fr)
    cs, ss = (cos * sq_rel).T, (sin * sq_rel).T
    ci, si = (cos / sq_rel).T, (sin / sq_rel).T
    QKS = 8.0
    bsq = np.empty(128, np.float64)
    bsq[0::2][:32] = base_ ** 2
    bsq[1::2][:32] = base_ ** 2
    bsq[64:] = bsq[:64]
    d["bsq"] = np.ascontiguousarray(bsq.astype(f32).reshape(128, 1))

    def mk(ev, od):
        a = np.empty((64, T))
        a[0::2] = ev
        a[1::2] = od
        return np.ascontiguousarray(np.tile(a, (2, 1)).astype(bf16))
    d["rcq"] = mk(cs * QKS, cs * QKS)
    d["rsq"] = mk(-ss * QKS, ss * QKS)
    d["rck"] = mk(ci, ci)
    d["rsk"] = mk(-si, si)

    r = np.arange(128)[:, None]
    c = np.arange(128)[None, :]
    d["mcur"] = np.ascontiguousarray((c >= r).astype(bf16))
    d["mprev"] = np.ascontiguousarray((c <= r).astype(bf16))
    e2 = np.zeros((128, 2), bf16)
    e2[:64, 0] = 1
    e2[64:, 1] = 1
    d["e2"] = e2
    d["onescol"] = np.ones((128, 1), bf16)
    ebo = np.zeros((16, 8 * 128), f32)
    for c2 in range(8):
        ebo[2 * c2, c2 * 128: c2 * 128 + 64] = 1.0
        ebo[2 * c2 + 1, c2 * 128 + 64: c2 * 128 + 128] = 1.0
    d["ebo"] = ebo
    ebi = np.zeros((8, 4 * 128), f32)
    for c2 in range(4):
        ebi[2 * c2, c2 * 128: c2 * 128 + 64] = 1.0
        ebi[2 * c2 + 1, c2 * 128 + 64: c2 * 128 + 128] = 1.0
    d["ebi"] = ebi
    ab = np.zeros((128, 4), f32)
    ab[:, 0] = float(np.asarray(inp["L0_attn_a"], f32))
    ab[:, 1] = float(np.asarray(inp["L0_attn_b"], f32))
    ab[:, 2] = float(np.asarray(inp["L0_attn_dt"], f32))
    d["abdt"] = ab
    return d



ENGS = ("tensor", "vector", "scalar", "gpsimd", "sync")


class Ctx:
    def __init__(self, nc, block, sems):
        self.nc = nc
        self.block = block
        self.pend = {e: [] for e in ENGS}
        self.sems = sems
        self.cnt = {n: 0 for n in sems}

    def op(self, eng, fn, waits=(), inc=None, inc_amt=1):
        waits = [(s, v) for (s, v) in waits if v > 0]

        def emit(e):
            for sname, val in waits:
                e.wait_ge(self.sems[sname], val)
            ins = fn(e)
            if inc is not None:
                ins.then_inc(self.sems[inc], inc_amt)
        self.pend[eng].append(emit)
        if inc is not None:
            self.cnt[inc] += inc_amt
            return self.cnt[inc]
        return None

    def dma(self, eng, out_ap, in_ap, waits=(), inc="dL"):
        return self.op(eng, lambda e: e.dma_start(out_ap, in_ap),
                       waits=waits, inc=inc, inc_amt=16)

    def flush(self):
        for eng in ENGS:
            ops = self.pend[eng]
            if not ops:
                continue
            self.pend[eng] = []

            def body(e, _ops=ops):
                for f in _ops:
                    f(e)
            getattr(self.block, eng)(body)

    def dbarrier(self):
        self.op("sync", lambda e: e.nop(),
                waits=[("dW", self.cnt["dW"]), ("dL", self.cnt["dL"]),
                       ("dO", self.cnt["dO"]), ("dH", self.cnt["dH"])])
        self.barrier()

    def barrier(self):
        if PARANOID:
            self.op("sync", lambda e: e.nop(),
                    waits=[("dW", self.cnt["dW"]), ("dL", self.cnt["dL"]),
                           ("dO", self.cnt["dO"]), ("dH", self.cnt["dH"])])
        tgt = self.cnt["bar"] + len(ENGS)
        for eng in ENGS:
            def emit(e, _t=tgt):
                e.nop().then_inc(self.sems["bar"], 1)
                e.wait_ge(self.sems["bar"], _t)
            self.pend[eng].append(emit)
        self.cnt["bar"] = tgt
        self.flush()


def mkap(t, offset, dims):
    return bass.AP(t.ap().tensor, offset, [list(x) for x in dims])



def build_program():
    nc = bass.Bass("TRN2", target_bir_lowering=False)

    def din(name, shape, dtype):
        return nc.dram_tensor(name, list(shape), dtype, kind="ExternalInput")

    xT = din("xT", (SPC, D, T), F32)
    wqkvT = din("wqkvT", (D, 3 * D), F32)
    woutT = din("woutT", (D, D), F32)
    iwqkv = din("iwqkv", (2, D, 1536), F32)
    iwo = din("iwo", (2, 512, D), F32)
    w1p = din("w1p", (2, D, 2 * HIDP), F32)
    w2p = din("w2p", (2, HIDP, D), F32)
    b1t = din("b1t", (128, 88), F32)
    b2t = din("b2t", (128, 16), F32)
    lngt = din("lngt", (128, 16), F32)
    lnbt = din("lnbt", (128, 16), F32)
    qst = din("qst", (128, 2), F32)
    kst = din("kst", (128, 2), F32)
    qswt = din("qswt", (128, 2), F32)
    kswt = din("kswt", (128, 2), F32)
    bsqt = din("bsq", (128, 1), F32)
    eboin = din("ebo", (16, 8 * 128), F32)
    ebiin = din("ebi", (8, 4 * 128), F32)
    rtabs = [din(n, (128, T), BF16) for n in ("rcq", "rsq", "rck", "rsk")]
    mcur = din("mcur", (128, 128), BF16)
    mprev = din("mprev", (128, 128), BF16)
    e2in = din("e2", (128, 2), BF16)
    onescol_in = din("onescol", (128, 1), BF16)
    abdt_in = din("abdt", (128, 4), F32)

    out_t = nc.dram_tensor("out", [D, SPC * T], F32, kind="ExternalOutput")
    dbg = {"kind": "ExternalOutput"} if (DEBUG_TAPS or DEBUG_LIGHT) else {}
    x1h = nc.dram_tensor("x1h", [D, SPC * T], F32, **dbg)
    zh = nc.dram_tensor("zh", [D, SPC * T], F32, **dbg)
    qsp = nc.dram_tensor("qsp", [D, T], BF16, **dbg)
    zsnap = (nc.dram_tensor("zsnap", [4, D, SPC * T], F32, kind="ExternalOutput")
             if (DEBUG_TAPS or DEBUG_LIGHT) else None)
    qkh = (nc.dram_tensor("qkh", [3, 128, 4 * T], BF16, kind="ExternalOutput")
           if DEBUG_TAPS else None)
    dbgE = (nc.dram_tensor("dbgE", [128, 2048], BF16, kind="ExternalOutput")
            if DEBUG_TAPS else None)
    dbgR = (nc.dram_tensor("dbgR", [128, 512 + 32], F32, kind="ExternalOutput")
            if DEBUG_TAPS else None)
    gh = nc.dram_tensor("gh", [16, 512], F32)
    mh = nc.dram_tensor("mh", [2, 16], F32)
    nrh = nc.dram_tensor("nrh", [2, 512], F32)

    es = ExitStack()

    _uid = [0]

    def salloc(stack, name, shape, dtype):
        _uid[0] += 1
        return stack.enter_context(
            nc.sbuf_tensor(f"{name}_{_uid[0]}", list(shape), dtype))

    sem_names = ["bar", "sT", "sV", "sS", "sG", "dL", "dW", "dO", "dH"]
    sems = {n: es.enter_context(nc.semaphore(n)) for n in sem_names}
    block = es.enter_context(nc.Block())
    C = Ctx(nc, block, sems)

    psum = es.enter_context(nc.psum_tensor("pool", [128, 4096], F32))

    def pbank(b, n=1):
        return psum[:, b * 512:(b + n) * 512]

    e2 = salloc(es, "e2s", (128, 2), BF16)
    onescol = salloc(es, "onescol_s", (128, 1), BF16)
    abdt = salloc(es, "abdt_s", (128, 4), F32)
    masks = salloc(es, "masks", (128, 256), BF16)
    qs_s = salloc(es, "qs_s", (128, 2), F32)
    ks_s = salloc(es, "ks_s", (128, 2), F32)
    qsw_s = salloc(es, "qsw_s", (128, 2), F32)
    ksw_s = salloc(es, "ksw_s", (128, 2), F32)
    bsq_s = salloc(es, "bsq_s", (128, 1), F32)
    ebo_s = salloc(es, "ebo_s", (80, 8 * 128), F32)
    ebi_s = salloc(es, "ebi_s", (72, 4 * 128), F32)
    b1_s = salloc(es, "b1_s", (128, 88), F32)
    b2_s = salloc(es, "b2_s", (128, 16), F32)
    lng_s = salloc(es, "lng_s", (128, 16), F32)
    lnb_s = salloc(es, "lnb_s", (128, 16), F32)
    xb_t = salloc(es, "xb_t", (128, 2 * 512), BF16)
    sqsc = salloc(es, "sqsc", (128, 2 * 512), BF16)
    sqn = salloc(es, "sqn", (128, 2 * 512), BF16)
    nrow = salloc(es, "nrow", (1, 1536), F32)
    nb_mu = salloc(es, "nb_mu", (128, 512), F32)
    nb_inv = salloc(es, "nb_inv", (128, 512), F32)
    nm = salloc(es, "nm", (2, 32), F32)
    rmax = salloc(es, "rmax", (2, 2), F32)
    mstage = salloc(es, "mstage", (2, 16), F32)
    bM = salloc(es, "bM", (128, 32), F32)
    cbias = salloc(es, "cbias", (128, 4), F32)
    C.op("gpsimd", lambda e: e.memset(cbias[:, 0:1], 1e-5))
    C.op("gpsimd", lambda e: e.memset(cbias[:, 1:2], 1e-34))
    C.op("gpsimd", lambda e: e.memset(cbias[:, 2:3], -5.0))
    C.op("gpsimd", lambda e: e.memset(cbias[:, 3:4], 0.0))

    for dst, src in ((e2, e2in), (onescol, onescol_in), (abdt, abdt_in),
                     (qs_s, qst), (ks_s, kst), (qsw_s, qswt), (ksw_s, kswt),
                     (bsq_s, bsqt), (b1_s, b1t), (b2_s, b2t), (lng_s, lngt),
                     (lnb_s, lnbt)):
        C.dma("sync", dst[:, :], src.ap()[:, :])
    C.dma("sync", ebo_s[64:80, :], eboin.ap()[:, :])
    C.dma("sync", ebi_s[64:72, :], ebiin.ap()[:, :])
    C.dma("sync", masks[:, 0:128], mcur.ap()[:, :])
    C.dma("sync", masks[:, 128:256], mprev.ap()[:, :])
    C.dbarrier()


    def ln_stats(src_of_c, bank1, bank2):
        p1 = psum[0:1, bank1 * 512:(bank1 + 1) * 512]
        p2 = psum[0:1, bank2 * 512:(bank2 + 1) * 512]
        t0_ = C.cnt["sT"]
        mmV = {0: t0_, 1: t0_}
        mmS = {0: t0_, 1: t0_}
        vT = t0_
        for c in range(NC):
            xs = src_of_c(c)
            sl = c % 2
            xb = xb_t[:, sl * 512:(sl + 1) * 512]
            sq = sqsc[:, sl * 512:(sl + 1) * 512]
            vV = C.op("vector", lambda e, xs=xs, xb=xb: e.tensor_copy(xb, xs),
                      waits=[("sT", mmV[sl])], inc="sV")
            mmV[sl] = C.op("tensor", lambda e, st=(c == 0), xb=xb: e.matmul(
                p1, onescol[:, :], xb, start=st, stop=(c == NC - 1)),
                waits=[("sV", vV)] + ([("sS", C.cnt["sS"])] if c == 0 else []),
                inc="sT")
            vS = C.op("scalar", lambda e, xs=xs, sq=sq: e.activation(
                sq, xs, AF.Square), waits=[("sT", mmS[sl])], inc="sS")
            vT = C.op("tensor", lambda e, st=(c == 0), sq=sq: e.matmul(
                p2, onescol[:, :], sq, start=st, stop=(c == NC - 1)),
                waits=[("sS", vS)], inc="sT")
            mmS[sl] = vT
        return vT

    def ln_rows_bcast(vT, bank1, bank2):
        p1 = psum[0:1, bank1 * 512:(bank1 + 1) * 512]
        p2 = psum[0:1, bank2 * 512:(bank2 + 1) * 512]
        r0 = nrow[0:1, 0:512]
        r1 = nrow[0:1, 512:1024]
        r2 = nrow[0:1, 1024:1536]
        C.op("vector", lambda e: e.tensor_copy(r0, p1),
             waits=[("sT", vT), ("dH", C.cnt["dH"])], inc="sV")
        C.op("vector", lambda e: e.tensor_copy(r1, p2))
        C.op("vector", lambda e: e.tensor_scalar(r0, r0, 1.0 / D, None, OP.mult))
        C.op("vector", lambda e: e.tensor_scalar(r1, r1, 1.0 / D, None, OP.mult))
        C.op("vector", lambda e: e.scalar_tensor_tensor(
            r2, r0, -1.0, r0, OP.mult, OP.mult))
        vV = C.op("vector", lambda e: e.tensor_tensor(r1, r1, r2, OP.add), inc="sV")
        vS = C.op("scalar", lambda e: e.activation(
            r1, r1, AF.Sqrt, bias=cbias[0:1, 0:1], scale=1.0),
            waits=[("sV", vV)], inc="sS")
        C.op("vector", lambda e: e.reciprocal(r1, r1), waits=[("sS", vS)])
        vV = C.op("vector", lambda e: e.tensor_scalar(
            r0, r0, -1.0, None, OP.mult), inc="sV")
        vO = C.dma("sync", mkap(nrh, 0, [[1024, 1], [1, 1024]]), nrow[0:1, 0:1024],
                   waits=[("sV", vV), ("sG", C.cnt["sG"])], inc="dH")
        C.dma("sync", nb_mu[:, :], mkap(nrh, 0, [[0, 128], [1, 512]]),
              waits=[("dH", vO)], inc="dH")
        vH = C.dma("sync", nb_inv[:, :], mkap(nrh, 512, [[0, 128], [1, 512]]),
                   inc="dH")
        return vH

    def neg_sqrt_bcast(ncols, vin):
        C.op("vector", lambda e: e.tensor_tensor(
            mstage[:, 0:ncols], nm[:, 0:ncols], nm[:, 16:16 + ncols], OP.mult),
            waits=[("sV", vin), ("dH", C.cnt["dH"])], inc="sV")
        vS = C.op("scalar", lambda e: e.activation(
            mstage[:, 0:ncols], mstage[:, 0:ncols], AF.Sqrt, bias=cbias[0:2, 1:2], scale=1.0),
            waits=[("sV", C.cnt["sV"])], inc="sS")
        vV = C.op("vector", lambda e: e.tensor_scalar(
            mstage[:, 0:ncols], mstage[:, 0:ncols], -1.0, EXPSHIFT,
            OP.mult, OP.add), waits=[("sS", vS)], inc="sV")
        vO = C.dma("sync", mh.ap()[:, 0:ncols], mstage[:, 0:ncols],
                   waits=[("sV", vV)], inc="dH")
        vH = C.op("sync", lambda e: e.dma_start(
            mkap(bM, 0, [[32, 128], [16, 2], [1, ncols]]),
            mkap(mh, 0, [[0, 128], [16, 2], [1, ncols]])),
            waits=[("dH", vO)], inc="dH", inc_amt=16)
        if FIXED_M:
            C.op("gpsimd", lambda e: e.memset(bM[:, :], 40.0),
                 waits=[("dH", vH)], inc="sG")
        return vH

    nrm_state = {"sl": 0, "mmS": {0: 0, 1: 0}, "rdV": {0: 0, 1: 0}}

    def norm_update(piece, col, extra_waits=()):
        sl = nrm_state["sl"]
        nrm_state["sl"] = 1 - sl
        sq = sqn[:, sl * 512:(sl + 1) * 512]
        pn = psum[0:2, (2 + sl) * 512:(3 + sl) * 512]
        rx = rmax[:, sl:sl + 1]
        vS = C.op("scalar", lambda e, sq=sq: e.activation(
            sq, piece, AF.Square),
            waits=list(extra_waits) + [("sT", nrm_state["mmS"][sl])], inc="sS")
        vT = C.op("tensor", lambda e, sq=sq, pn=pn: e.matmul(
            pn, e2[:, :], sq, start=True, stop=True),
            waits=[("sS", vS), ("sV", nrm_state["rdV"][sl])], inc="sT")
        nrm_state["mmS"][sl] = vT
        vV = C.op("vector", lambda e, pn=pn, rx=rx: e.reduce_max(
            rx, pn, axis=AX.X), waits=[("sT", vT)], inc="sV")
        nrm_state["rdV"][sl] = vV
        vV = C.op("vector", lambda e, rx=rx, col=col: e.tensor_tensor(
            nm[:, col:col + 1], nm[:, col:col + 1], rx, OP.max), inc="sV")
        return vV

    sS0 = ExitStack()
    wqkv_sb = salloc(sS0, "wqkv_sb", (128, NC * 3 * D), BF16)
    kfm = salloc(sS0, "kfm", (128, NC * T), BF16)
    vsb = salloc(sS0, "vsb", (128, 16 * 1280), BF16)
    for k in range(NC):
        C.dma("gpsimd", wqkv_sb[:, k * 3 * D:(k + 1) * 3 * D],
              wqkvT.ap()[k * 128:(k + 1) * 128, :],
              waits=[("dO", C.cnt["dO"]), ("dH", C.cnt["dH"]),
                     ("dL", C.cnt["dL"])] if k == 0 else (), inc="dW")
    for tile in range(16):
        base = tile * 1280
        for h in range(H_OUT):
            C.op("gpsimd", lambda e, b=base, hh=h: e.memset(
                vsb[:, b + hh * 80 + 64: b + hh * 80 + 80], 0.0))
        for h in range(H_OUT):
            C.op("gpsimd", lambda e, b=base, hh=h: e.memset(
                vsb[:, b + hh * 80 + 64 + hh: b + hh * 80 + 64 + hh + 1], 1.0))
    C.op("gpsimd", lambda e: e.memset(nm[:, :], 0.0))
    C.dbarrier()

    for seq in range(SPC):
        sA = ExitStack()
        x_t = salloc(sA, "x_t", (128, NC * 512), F32)
        y_t = salloc(sA, "y_t", (128, NC * 512), BF16)
        qpiece = salloc(sA, "qpiece", (128, 2 * 512), BF16)
        C.barrier()
        evacV = {0: 0, 1: 0}
        spillO = {0: 0, 1: 0}
        for ti in range(NT):
            t0 = ti * 512
            vL = 0
            for c in range(NC):
                vL = C.dma("sync", x_t[:, c * 512:(c + 1) * 512],
                           mkap(xT, seq * D * T + c * 128 * T + t0,
                                [[T, 128], [1, 512]]),
                           waits=[("dO", C.cnt["dO"]), ("sV", C.cnt["sV"]),
                                  ("sS", C.cnt["sS"]), ("sG", C.cnt["sG"])]
                           if c == 0 else ())
            C.pend["vector"].append(lambda e, v=vL: e.wait_ge(sems["dL"], v))
            C.pend["scalar"].append(lambda e, v=vL: e.wait_ge(sems["dL"], v))
            vT = ln_stats(lambda c: x_t[:, c * 512:(c + 1) * 512], 0, 1)
            vH = ln_rows_bcast(vT, 0, 1)
            vY = 0
            for c in range(NC):
                xs = x_t[:, c * 512:(c + 1) * 512]
                ys = y_t[:, c * 512:(c + 1) * 512]
                vG = C.op("gpsimd", lambda e, xs=xs: e.tensor_tensor(
                    xs, xs, nb_mu[:, :], OP.add), waits=[("dH", vH)], inc="sG")
                vY = C.op("vector", lambda e, xs=xs, ys=ys: e.tensor_tensor(
                    ys, xs, nb_inv[:, :], OP.mult),
                    waits=[("sG", vG)] + ([("sT", C.cnt["sT"])] if c == 0 else []),
                    inc="sV")
            for n in range(16):
                bank = n % 2
                po = pbank(bank)
                for k in range(NC):
                    vT = C.op("tensor", lambda e, n=n, k=k, po=po, st=(k == 0):
                              e.matmul(
                        po, wqkv_sb[:, k * 3 * D + n * 128: k * 3 * D + (n + 1) * 128],
                        y_t[:, k * 512:(k + 1) * 512], start=st, stop=(k == NC - 1)),
                        waits=([("sV", vY), ("sV", evacV[bank])] if k == 0 else ()),
                        inc="sT")
                if n < 8:
                    dst = qpiece[:, bank * 512: bank * 512 + 512]
                    vV = C.op("vector", lambda e, po=po, dst=dst: e.tensor_scalar(
                        dst, po, 0.125, None, OP.mult),
                        waits=[("sT", vT), ("dO", spillO[bank]),
                               ("sS", C.cnt["sS"])], inc="sV")
                    spillO[bank] = C.dma(
                        "sync", mkap(qsp, n * 128 * T + t0, [[T, 128], [1, 512]]),
                        dst, waits=[("sV", vV)], inc="dO")
                    piece = dst
                else:
                    piece = kfm[:, (n - 8) * T + t0: (n - 8) * T + t0 + 512]
                    vV = C.op("vector", lambda e, po=po, p=piece: e.tensor_copy(p, po),
                              waits=[("sT", vT)], inc="sV")
                evacV[bank] = vV
                norm_update(piece, (0 if n < 8 else 16) + (n % 8),
                            extra_waits=[("sV", vV)])
            for tt in range(4):
                tok0 = tt * 128
                vtile = ti * 4 + tt
                for half in range(2):
                    bank = (tt * 2 + half) % 2
                    po = pbank(bank)
                    for k in range(NC):
                        vT = C.op("tensor", lambda e, k=k, half=half, tok0=tok0,
                                  po=po, st=(k == 0): e.matmul(
                            po, y_t[:, k * 512 + tok0: k * 512 + tok0 + 128],
                            wqkv_sb[:, k * 3 * D + 2048 + half * 512:
                                    k * 3 * D + 2048 + (half + 1) * 512],
                            start=st, stop=(k == NC - 1)),
                            waits=[("sV", evacV[bank])] if k == 0 else (), inc="sT")
                    vV = 0
                    for hh in range(8):
                        h = half * 8 + hh
                        vV = C.op("vector", lambda e, po=po, vt=vtile, h=h, hh=hh:
                                  e.tensor_copy(
                            vsb[:, vt * 1280 + h * 80: vt * 1280 + h * 80 + 64],
                            po[:, hh * 64: hh * 64 + 64]),
                            waits=[("sT", vT)] if hh == 0 else (), inc="sV")
                    evacV[bank] = vV
        C.barrier()
        sA.close()

        vHM = neg_sqrt_bcast(8, 0)
        C.op("gpsimd", lambda e: e.memset(nm[:, :], 0.0), waits=[("dH", vHM)])
        sB = ExitStack()
        wout_sb = salloc(sB, "wout_sb", (128, NC * D), BF16)
        qt = salloc(sB, "qt", (128, NC * 512), BF16)
        exps = salloc(sB, "exps", (128, 2048), BF16)
        ao = salloc(sB, "ao", (128, NC * 512), BF16)
        stage64 = salloc(sB, "stage64", (64, 2 * 512), BF16)
        fhn = salloc(sB, "fhn", (80, 512), F32)
        fx = salloc(sB, "fx", (80, 6 * 512), F32)
        fsc = salloc(sB, "fsc", (80, 8), F32)
        gb = salloc(sB, "gb", (128, 2 * 512), F32)
        x1t = salloc(sB, "x1t", (128, 2 * 512), F32)
        for k in range(NC):
            C.dma("gpsimd", wout_sb[:, k * D:(k + 1) * D],
                  woutT.ap()[k * 128:(k + 1) * 128, :],
                  waits=[("dO", C.cnt["dO"]), ("dH", C.cnt["dH"]),
                         ("dL", C.cnt["dL"])] if k == 0 else (), inc="dW")
        C.dbarrier()

        expW = {0: 0, 1: 0}
        opEv = {0: 0, 1: 0}
        opSt = {0: 0, 1: 0}
        avEv = {0: 0, 1: 0}
        stageO = {0: 0, 1: 0}
        vGb = {0: 0, 1: 0}
        for ci in range(4):
            t0 = ci * 512
            vQ = 0
            for c in range(NC):
                vQ = C.dma("sync", qt[:, c * 512:(c + 1) * 512],
                           mkap(qsp, c * 128 * T + t0, [[T, 128], [1, 512]]),
                           waits=[("dO", C.cnt["dO"]), ("sT", C.cnt["sT"])]
                           if c == 0 else ())
            C.op("gpsimd", lambda e: e.memset(fhn[64:80, :], 0.0),
                 waits=[("sV", C.cnt["sV"]), ("dH", C.cnt["dH"])], inc="sG")
            vFz = C.cnt["sG"]
            for h in range(H_OUT):
                c, hp = h // 2, h % 2
                avb = h % 2
                pav = psum[0:80, (4 + avb) * 512:(5 + avb) * 512]
                bcol = (h % 2) * 16 + h // 2
                for g in range(ci + 1):
                    for half in range(2):
                        for sc in (0, 1) if half == 0 else (2, 3):
                            kc0 = g * 512 + sc * 128
                            vT = C.op("tensor", lambda e, c=c, hp=hp, kc0=kc0,
                                      sc=sc: e.matmul(
                                psum[:, sc * 512:(sc + 1) * 512],
                                kfm[hp * 64:(hp + 1) * 64,
                                    c * T + kc0: c * T + kc0 + 128],
                                qt[hp * 64:(hp + 1) * 64, c * 512: c * 512 + 512],
                                start=True, stop=True),
                                waits=([("dL", vQ), ("sS", expW[half])]
                                       if sc % 2 == 0 else ()), inc="sT")
                        vS = C.op("scalar", lambda e, bcol=bcol, half=half:
                                  e.activation(
                            exps[:, half * 1024:(half + 1) * 1024],
                            psum[:, half * 1024:(half + 1) * 1024], AF.Exp,
                            bias=bM[:, bcol:bcol + 1], scale=1.0),
                            waits=[("sT", vT)], inc="sS")
                        expW[half] = vS
                        for sc in (0, 1) if half == 0 else (2, 3):
                            ktile = g * 4 + sc
                            st = (g == 0 and sc == 0)
                            vT = C.op("tensor", lambda e, kt=ktile, h=h, pav=pav,
                                      sc=sc, st=st: e.matmul(
                                pav,
                                vsb[:, kt * 1280 + h * 80: kt * 1280 + h * 80 + 80],
                                exps[:, sc * 512:(sc + 1) * 512],
                                start=st, stop=False),
                                waits=([("sS", vS)] if sc % 2 == 0 else []) +
                                      ([("sV", avEv[avb])] if st else []), inc="sT")
                vT = C.op("tensor", lambda e: e.nop(), inc="sT")
                if hp == 0:
                    vV = C.op("vector", lambda e, pav=pav, c=c: e.tensor_copy(
                        ao[0:64, c * 512:(c + 1) * 512], pav[0:64, :]),
                        waits=[("sT", vT)], inc="sV")
                else:
                    sg = stage64[:, (c % 2) * 512:(c % 2) * 512 + 512]
                    vV = C.op("vector", lambda e, pav=pav, sg=sg: e.tensor_copy(
                        sg, pav[0:64, :]),
                        waits=[("sT", vT), ("dO", stageO[c % 2])], inc="sV")
                    stageO[c % 2] = C.dma(
                        "sync", ao[64:128, c * 512:(c + 1) * 512], sg,
                        waits=[("sV", vV)], inc="dO")
                vV = C.op("vector", lambda e, pav=pav: e.tensor_tensor(
                    fhn[64:80, :], fhn[64:80, :], pav[64:80, :], OP.add),
                    waits=[("sG", vFz)], inc="sV")
                avEv[avb] = vV
            a_ap = abdt[64:80, 0:1]
            b_ap = abdt[64:80, 1:2]
            dt_ap = abdt[64:80, 2:3]
            r_ = fhn[64:80, :]

            def fs(i):
                return fx[64:80, i * 512:(i + 1) * 512]
            rec, S_, tmp, I_, vv, ww = fs(0), fs(1), fs(2), fs(3), fs(4), fs(5)
            C.op("vector", lambda e: e.tensor_scalar(
                r_, r_, 1e-37, None, OP.max),
                waits=[("dH", C.cnt["dH"])], inc="sV")
            al = fsc[64:80, 0:1]
            ad = fsc[64:80, 1:2]
            smax = fsc[64:80, 2:3]
            C.op("vector", lambda e: e.reciprocal(rec, r_), inc="sV")
            C.op("vector", lambda e: e.tensor_tensor(S_, r_, rec, OP.mult))
            C.op("vector", lambda e: e.tensor_scalar(al, dt_ap, 1.0 / 12.5, None, OP.mult))
            C.op("vector", lambda e: e.tensor_tensor(ad, al, b_ap, OP.mult))
            C.op("vector", lambda e: e.tensor_scalar(ad, ad, 1.0, None, OP.add))
            C.op("vector", lambda e: e.reciprocal(ad, ad))
            C.op("vector", lambda e: e.tensor_tensor(ad, ad, al, OP.mult))
            vV = C.op("vector", lambda e: e.tensor_scalar(
                tmp, S_, 1e-6, None, OP.max), inc="sV")
            C.op("vector", lambda e: e.reciprocal(I_, tmp))
            C.op("vector", lambda e: e.tensor_tensor(I_, S_, I_, OP.mult))
            vS = C.op("scalar", lambda e: e.activation(
                vv, S_, AF.Sigmoid, bias=cbias[64:80, 2:3], scale=10.0),
                waits=[("sV", vV)], inc="sS")
            C.op("vector", lambda e: e.tensor_scalar(
                vv, vv, 0.9, 0.1, OP.mult, OP.add), waits=[("sS", vS)])
            C.op("vector", lambda e: e.tensor_tensor(I_, I_, vv, OP.mult))
            C.op("vector", lambda e: e.tensor_scalar(vv, I_, dt_ap, None, OP.mult))
            C.op("vector", lambda e: e.tensor_scalar(vv, vv, 3.0, -3.0, OP.min, OP.max))
            C.op("vector", lambda e: e.tensor_scalar(ww, vv, a_ap, ad, OP.add, OP.mult))
            C.op("vector", lambda e: e.tensor_scalar(ww, ww, 3.0, -3.0, OP.min, OP.max))
            x3 = rec
            x3 = fx[64:80, 5 * 512:6 * 512]
            t3 = fx[64:80, 2 * 512:3 * 512]
            C.op("vector", lambda e: e.tensor_tensor(t3, vv, vv, OP.mult))
            C.op("vector", lambda e: e.tensor_tensor(t3, t3, vv, OP.mult))
            C.op("vector", lambda e: e.scalar_tensor_tensor(
                t3, t3, -1.0 / 3.0, vv, OP.mult, OP.add))
            C.op("vector", lambda e: e.tensor_tensor(t3, t3, ww, OP.subtract))
            C.op("vector", lambda e: e.tensor_tensor(t3, t3, I_, OP.add))
            C.op("vector", lambda e: e.scalar_tensor_tensor(
                vv, t3, dt_ap, vv, OP.mult, OP.add))
            C.op("vector", lambda e: e.tensor_scalar(vv, vv, 3.0, -3.0, OP.min, OP.max))
            C.op("vector", lambda e: e.tensor_scalar(t3, S_, 1e-6, None, OP.max))
            vV = C.op("vector", lambda e: e.tensor_tensor(t3, vv, t3, OP.mult), inc="sV")
            vS = C.op("scalar", lambda e: e.activation(t3, t3, AF.Sigmoid),
                      waits=[("sV", vV)], inc="sS")
            C.op("vector", lambda e: e.tensor_scalar(
                t3, t3, 0.5, 0.5, OP.mult, OP.add), waits=[("sS", vS)])
            C.op("vector", lambda e: e.tensor_tensor(I_, t3, S_, OP.mult))
            C.op("vector", lambda e: e.tensor_scalar(I_, I_, 1e-8, None, OP.add))
            C.op("vector", lambda e: e.reciprocal(I_, I_))
            C.op("vector", lambda e: e.tensor_tensor(I_, I_, t3, OP.mult))
            vV = C.op("vector", lambda e: e.tensor_tensor(I_, I_, rec, OP.mult),
                      inc="sV")
            vSc = 0
            for c2 in range(NC):
                bank = 6 + c2 % 2
                pg = pbank(bank)
                vT = C.op("tensor", lambda e, c2=c2, pg=pg: e.matmul(
                    pg, ebo_s[64:80, c2 * 128:(c2 + 1) * 128], I_,
                    start=True, stop=True),
                    waits=[("sV", vV), ("sV", opEv[c2 % 2]),
                           ("sV", vGb[c2 % 2])], inc="sT")
                vSc = C.op("vector", lambda e, c2=c2, pg=pg: e.tensor_tensor(
                    ao[:, c2 * 512:(c2 + 1) * 512], ao[:, c2 * 512:(c2 + 1) * 512],
                    pg, OP.mult),
                    waits=[("sT", vT), ("dO", C.cnt["dO"])], inc="sV")
                vGb[c2 % 2] = vSc
            for fo in range(NC):
                b2_ = fo % 2
                po = pbank(6 + b2_)
                x1s = x1t[:, b2_ * 512:(b2_ + 1) * 512]
                for k in range(NC):
                    vT = C.op("tensor", lambda e, fo=fo, k=k, po=po, st=(k == 0):
                              e.matmul(
                        po, wout_sb[:, k * D + fo * 128: k * D + (fo + 1) * 128],
                        ao[:, k * 512:(k + 1) * 512], start=st, stop=(k == NC - 1)),
                        waits=[("sV", vSc), ("sV", opEv[b2_])] if k == 0 else (),
                        inc="sT")
                vX = C.dma("sync", x1s,
                           mkap(xT, seq * D * T + fo * 128 * T + t0,
                                [[T, 128], [1, 512]]),
                           waits=[("sV", opEv[b2_]), ("dO", opSt[b2_])], inc="dL")
                vE = C.op("vector", lambda e, po=po, x1s=x1s: e.tensor_tensor(
                    x1s, po, x1s, OP.add),
                    waits=[("sT", vT), ("dL", vX)], inc="sV")
                opEv[b2_] = vE
                opSt[b2_] = C.dma(
                    "sync", x1h.ap()[fo * 128:(fo + 1) * 128,
                                     seq * T + t0: seq * T + t0 + 512],
                    x1s, waits=[("sV", vE)], inc="dO")
            C.barrier()
        sB.close()
    C.barrier()
    sS0.close()

    sS1 = ExitStack()
    x_t = salloc(sS1, "x_t1", (128, NC * 512), F32)
    C.barrier()
    for seq in range(SPC):
        for ti in range(NT):
            t0 = ti * 512
            vL = 0
            for c in range(NC):
                vL = C.dma("sync", x_t[:, c * 512:(c + 1) * 512],
                           x1h.ap()[c * 128:(c + 1) * 128,
                                    seq * T + t0: seq * T + t0 + 512],
                           waits=[("dO", C.cnt["dO"]), ("sV", C.cnt["sV"]),
                                  ("sS", C.cnt["sS"]), ("sG", C.cnt["sG"])]
                           if c == 0 else ())
            C.pend["vector"].append(lambda e, v=vL: e.wait_ge(sems["dL"], v))
            C.pend["scalar"].append(lambda e, v=vL: e.wait_ge(sems["dL"], v))
            vT = ln_stats(lambda c: x_t[:, c * 512:(c + 1) * 512], 0, 1)
            vH = ln_rows_bcast(vT, 0, 1)
            for c in range(NC):
                xs = x_t[:, c * 512:(c + 1) * 512]
                vG = C.op("gpsimd", lambda e, xs=xs: e.tensor_tensor(
                    xs, xs, nb_mu[:, :], OP.add), waits=[("dH", vH)], inc="sG")
                vE = C.op("vector", lambda e, xs=xs: e.tensor_tensor(
                    xs, xs, nb_inv[:, :], OP.mult), waits=[("sG", vG)], inc="sV")
                C.dma("sync", zh.ap()[c * 128:(c + 1) * 128,
                                      seq * T + t0: seq * T + t0 + 512],
                      xs, waits=[("sV", vE)], inc="dO")
            C.barrier()
    sS1.close()

    sS2 = ExitStack()
    zt = salloc(sS2, "zt", (128, NC * 513), F32)
    C.barrier()
    _snap_i = [0]

    def zsnap_take(i):
        if zsnap is None:
            return
        C.dma("sync", mkap(zsnap, i * D * SPC * T, [[SPC * T, D], [1, SPC * T]]),
              mkap(zh, 0, [[SPC * T, D], [1, SPC * T]]),
              waits=[("dO", C.cnt["dO"])], inc="dO")

    def zload_pre(seq, ti, hbuf, vHb):
        t0 = ti * 512
        vL = 0
        ew = [("dO", C.cnt["dO"]), ("sV", C.cnt["sV"]), ("sS", C.cnt["sS"]),
              ("sG", C.cnt["sG"])]
        for c in range(NC):
            dst = zt[:, c * 513:(c + 1) * 513]
            vL = C.dma("sync", dst[:, 1:513],
                       zh.ap()[c * 128:(c + 1) * 128,
                               seq * T + t0: seq * T + t0 + 512],
                       waits=ew if c == 0 else ())
        w0 = [("dO", C.cnt["dO"]), ("sV", C.cnt["sV"]), ("sS", C.cnt["sS"])]
        if ti == 0:
            for c in range(NC):
                C.op("gpsimd", lambda e, c=c: e.memset(
                    zt[:, c * 513: c * 513 + 1], 0.0),
                    waits=w0 if c == 0 else (), inc="sG")
        else:
            for c in range(NC):
                C.op("gpsimd", lambda e, c=c, ti=ti: e.tensor_copy(
                    zt[:, c * 513: c * 513 + 1],
                    hbuf[:, c * 4 + ti: c * 4 + ti + 1]),
                    waits=(w0 + [("dL", vHb)]) if c == 0 else (), inc="sG")
        return vL, C.cnt["sG"]

    def zload(seq, ti, halo):
        t0 = ti * 512
        vL = 0
        ew = [("dO", C.cnt["dO"]), ("sV", C.cnt["sV"]), ("sS", C.cnt["sS"]),
              ("sG", C.cnt["sG"])]
        for c in range(NC):
            dst = zt[:, c * 513:(c + 1) * 513]
            if halo and ti == 0:
                vL = C.dma("sync", dst[:, 1:513],
                           zh.ap()[c * 128:(c + 1) * 128, seq * T: seq * T + 512],
                           waits=ew if c == 0 else ())
                C.op("gpsimd", lambda e, dst=dst: e.memset(dst[:, 0:1], 0.0),
                     waits=[("dO", C.cnt["dO"])], inc="sG")
            elif halo:
                vL = C.dma("sync", dst,
                           zh.ap()[c * 128:(c + 1) * 128,
                                   seq * T + t0 - 1: seq * T + t0 + 512],
                           waits=ew if c == 0 else ())
            else:
                vL = C.dma("sync", dst[:, 0:512],
                           zh.ap()[c * 128:(c + 1) * 128,
                                   seq * T + t0: seq * T + t0 + 512],
                           waits=ew if c == 0 else ())
        return vL, C.cnt["sG"]

    zsnap_take(0)
    for lay in range(2):
        s2a = ExitStack()
        iwq_sb = salloc(s2a, "iwq_sb", (128, NC * 1536), BF16)
        iwo_sb = salloc(s2a, "iwo_sb", (128, 4 * D), BF16)
        rt_fold = salloc(s2a, "rt_fold", (128, 4 * T), BF16)
        qf = salloc(s2a, "qf", (128, 4 * T), BF16)
        kf = salloc(s2a, "kf", (128, 4 * T), BF16)
        kf2 = salloc(s2a, "kf2", (128, 4 * T), BF16)
        ivsb = salloc(s2a, "ivsb", (128, 16 * 576), BF16)
        ht = salloc(s2a, "ht", (128, NC * 512), BF16)
        praw = salloc(s2a, "praw", (128, 512), F32)
        prep1 = salloc(s2a, "prep1", (128, 512), F32)
        prep2 = salloc(s2a, "prep2", (128, 512), F32)
        prep3 = salloc(s2a, "prep3", (128, 512), F32)
        rnb = salloc(s2a, "rnb", (128, 512), F32)
        nrow2 = salloc(s2a, "nrow2", (2, 512), F32)
        exps = salloc(s2a, "exps", (128, 2048), BF16)
        oat = salloc(s2a, "oat", (128, 4 * 512), BF16)
        stage64 = salloc(s2a, "stage64i", (64, 2 * 512), BF16)
        giw = salloc(s2a, "giw", (72, 512), F32)
        gbi = salloc(s2a, "gbi", (128, 2 * 512), F32)
        C.barrier()
        for k in range(NC):
            C.dma("gpsimd", iwq_sb[:, k * 1536:(k + 1) * 1536],
                  mkap(iwqkv, lay * D * 1536 + k * 128 * 1536, [[1536, 128], [1, 1536]]),
                  waits=[("dO", C.cnt["dO"]), ("dH", C.cnt["dH"]),
                         ("dL", C.cnt["dL"])] if k == 0 else (), inc="dW")
        for k in range(4):
            C.dma("gpsimd", iwo_sb[:, k * D:(k + 1) * D],
                  mkap(iwo, lay * 512 * D + k * 128 * D, [[D, 128], [1, D]]),
                  inc="dW")
        vW = 0
        for i, tab in enumerate(rtabs):
            vW = C.dma("sync", rt_fold[:, i * T:(i + 1) * T], tab.ap()[:, :])
        for i, scv in enumerate((qs_s, qsw_s, ks_s, ksw_s)):
            sc_ap = scv[:, lay:lay + 1]
            C.op("vector", lambda e, i=i, sc_ap=sc_ap: e.tensor_scalar(
                rt_fold[:, i * T:(i + 1) * T], rt_fold[:, i * T:(i + 1) * T],
                sc_ap, None, OP.mult), waits=[("dL", vW)], inc="sV")
        C.op("gpsimd", lambda e: e.memset(nm[:, :], 0.0))
        for tile in range(16):
            base = tile * 576
            for h in range(H_IN):
                C.op("gpsimd", lambda e, b=base, hh=h: e.memset(
                    ivsb[:, b + hh * 72 + 64: b + hh * 72 + 72], 0.0))
            for h in range(H_IN):
                C.op("gpsimd", lambda e, b=base, hh=h: e.memset(
                    ivsb[:, b + hh * 72 + 64 + hh: b + hh * 72 + 64 + hh + 1], 1.0))
        C.dbarrier()

        for seq in range(SPC):
            C.barrier()
            evacV = {0: 0, 1: 0}
            lastG = 0
            for ti in range(NT):
                t0 = ti * 512
                vL, vG0 = zload(seq, ti, halo=True)
                vHt = 0
                for c in range(NC):
                    src = (zt[:, c * 513 + 1: c * 513 + 513] if c < 4 else
                           zt[:, c * 513 + 0: c * 513 + 512])
                    vHt = C.op("vector", lambda e, c=c, src=src: e.tensor_copy(
                        ht[:, c * 512:(c + 1) * 512], src),
                        waits=[("dL", vL), ("sG", vG0), ("sT", C.cnt["sT"])],
                        inc="sV")
                for n in range(8):
                    bank = n % 2
                    po = pbank(bank)
                    for k in range(NC):
                        vT = C.op("tensor", lambda e, n=n, k=k, po=po, st=(k == 0):
                                  e.matmul(
                            po, iwq_sb[:, k * 1536 + n * 128: k * 1536 + (n + 1) * 128],
                            ht[:, k * 512:(k + 1) * 512], start=st,
                            stop=(k == NC - 1)),
                            waits=([("sV", vHt), ("sV", evacV[bank])] if k == 0
                                   else ()), inc="sT")
                    vE = C.op("vector", lambda e, po=po: e.tensor_copy(praw[:, :], po),
                              waits=[("sT", vT), ("sG", lastG),
                                     ("sS", C.cnt["sS"])], inc="sV")
                    evacV[bank] = vE
                    isq = n < 4
                    chunk = n if isq else n - 4
                    dst = (qf if isq else kf)[:, chunk * T + t0: chunk * T + t0 + 512]
                    tabC = rt_fold[:, (0 if isq else 2) * T + t0:
                                   (0 if isq else 2) * T + t0 + 512]
                    tabS = rt_fold[:, (1 if isq else 3) * T + t0:
                                   (1 if isq else 3) * T + t0 + 512]
                    vS = C.op("scalar", lambda e: e.activation(
                        sqsc[:, 0:512], praw[:, :], AF.Square),
                        waits=[("sV", vE), ("sT", C.cnt["sT"])], inc="sS")
                    vT2 = C.op("tensor", lambda e: e.matmul(
                        psum[0:2, 7 * 512:8 * 512], e2[:, :], sqsc[:, 0:512],
                        start=True, stop=True),
                        waits=[("sS", vS), ("sV", C.cnt["sV"])], inc="sT")
                    vV = C.op("vector", lambda e: e.tensor_copy(
                        nrow2[:, :], psum[0:2, 7 * 512:8 * 512]),
                        waits=[("sT", vT2)], inc="sV")
                    vS = C.op("scalar", lambda e: e.activation(
                        nrow2[:, :], nrow2[:, :], AF.Sqrt, bias=cbias[0:2, 1:2], scale=1.0),
                        waits=[("sV", vV)], inc="sS")
                    C.op("vector", lambda e: e.tensor_scalar(
                        nrow2[:, :], nrow2[:, :], 1e-12, None, OP.max),
                        waits=[("sS", vS)])
                    vV = C.op("vector", lambda e: e.reciprocal(
                        nrow2[:, :], nrow2[:, :]), inc="sV")
                    vO = C.dma("sync", nrh.ap()[0:2, :], nrow2[:, :],
                               waits=[("sV", vV), ("dH", C.cnt["dH"]),
                                      ("sG", C.cnt["sG"])], inc="dH")
                    C.dma("sync", rnb[0:64, :], mkap(nrh, 0, [[0, 64], [1, 512]]),
                          waits=[("dH", vO)], inc="dH")
                    vH1 = C.dma("sync", rnb[64:128, :],
                                mkap(nrh, 512, [[0, 64], [1, 512]]), inc="dH")
                    vG = C.op("gpsimd", lambda e: e.tensor_tensor(
                        prep1[:, :], praw[:, :], rnb[:, :], OP.mult),
                        waits=[("dH", vH1), ("sV", C.cnt["sV"])], inc="sG")
                    vG = C.op("gpsimd", lambda e, tabC=tabC: e.tensor_tensor(
                        prep2[:, :], prep1[:, :], tabC, OP.mult), inc="sG")
                    vV = C.op("vector", lambda e: e.stream_shuffle(
                        prep3[:, :], prep1[:, :], SWAPMASK), waits=[("sG", vG)],
                        inc="sV")
                    vG = C.op("gpsimd", lambda e, tabS=tabS: e.tensor_tensor(
                        prep3[:, :], prep3[:, :], tabS, OP.mult),
                        waits=[("sV", vV)], inc="sG")
                    vG = C.op("gpsimd", lambda e, dst=dst: e.tensor_tensor(
                        dst, prep2[:, :], prep3[:, :], OP.add), inc="sG")
                    lastG = vG
                    if not isq:
                        dst2 = kf2[:, chunk * T + t0: chunk * T + t0 + 512]
                        vG = C.op("gpsimd", lambda e, dst=dst, dst2=dst2:
                                  e.tensor_scalar(
                            dst2, dst, bsq_s[:, 0:1], None, OP.mult), inc="sG")
                        lastG = vG
                    norm_update(dst, (0 if isq else 16) + chunk * 4 + ti,
                                extra_waits=[("sG", vG)])
                for tt in range(4):
                    tok0 = tt * 128
                    vtile = ti * 4 + tt
                    bank = tt % 2
                    po = pbank(bank)
                    for k in range(NC):
                        vT = C.op("tensor", lambda e, k=k, tok0=tok0, po=po,
                                  st=(k == 0): e.matmul(
                            po, ht[:, k * 512 + tok0: k * 512 + tok0 + 128],
                            iwq_sb[:, k * 1536 + 1024:(k + 1) * 1536],
                            start=st, stop=(k == NC - 1)),
                            waits=[("sV", evacV[bank])] if k == 0 else (), inc="sT")
                    vV = 0
                    for h in range(H_IN):
                        vV = C.op("vector", lambda e, po=po, vt=vtile, h=h:
                                  e.tensor_copy(
                            ivsb[:, vt * 576 + h * 72: vt * 576 + h * 72 + 64],
                            po[:, h * 64: h * 64 + 64]),
                            waits=[("sT", vT)] if h == 0 else (), inc="sV")
                    evacV[bank] = vV
            C.barrier()

            C.op("vector", lambda e: e.tensor_copy(
                mstage[:, 0:16], nm[:, 16:32]),
                waits=[("dH", C.cnt["dH"])], inc="sV")
            for w_ in range(1, 4):
                C.op("vector", lambda e, w_=w_: e.tensor_tensor(
                    mkap(mstage, w_, [[16, 2], [4, 4]]),
                    mkap(mstage, w_, [[16, 2], [4, 4]]),
                    mkap(nm, 16 + w_ - 1, [[32, 2], [4, 4]]), OP.max))
            vV = C.op("vector", lambda e: e.tensor_tensor(
                mstage[:, 0:16], mstage[:, 0:16], nm[:, 0:16], OP.mult), inc="sV")
            vS = C.op("scalar", lambda e: e.activation(
                mstage[:, 0:16], mstage[:, 0:16], AF.Sqrt, bias=cbias[0:2, 1:2],
                scale=1.0), waits=[("sV", vV)], inc="sS")
            vV = C.op("vector", lambda e: e.tensor_scalar(
                mstage[:, 0:16], mstage[:, 0:16], -1.0, EXPSHIFT,
                OP.mult, OP.add), waits=[("sS", vS)], inc="sV")
            vO = C.dma("sync", mh.ap()[:, :], mstage[:, 0:16],
                       waits=[("sV", vV)], inc="dH")
            vHM = C.op("sync", lambda e: e.dma_start(
                mkap(bM, 0, [[32, 128], [16, 2], [1, 16]]),
                mkap(mh, 0, [[0, 128], [16, 2], [1, 16]])),
                waits=[("dH", vO)], inc="dH", inc_amt=16)
            if FIXED_M:
                C.op("gpsimd", lambda e: e.memset(bM[:, :], 0.0),
                     waits=[("dH", vHM)], inc="sG")
            C.op("gpsimd", lambda e: e.memset(nm[:, :], 0.0), waits=[("dH", vHM)])
            C.barrier()

            opEv = {0: 0, 1: 0}
            avEv = {0: 0, 1: 0}
            stageO = {0: 0, 1: 0}
            vGb = {0: 0, 1: 0}
            for w in range(4):
                t0 = w * 512
                C.op("gpsimd", lambda e: e.memset(giw[64:72, :], 0.0),
                     waits=[("sV", C.cnt["sV"]), ("dH", C.cnt["dH"])], inc="sG")
                vFz = C.cnt["sG"]
                for h in range(H_IN):
                    c, hp = h // 2, h % 2
                    avb = h % 2
                    pav = psum[0:72, (4 + avb) * 512:(5 + avb) * 512]
                    sources = ([("prev", w - 1)] if w > 0 else []) + [("cur", w)]
                    first = True
                    for typ, wsrc in sources:
                        vSs = []
                        geo = []
                        for sc in range(4):
                            if typ == "prev":
                                qoff, N = 0, 128 * (sc + 1)
                            else:
                                qoff, N = 128 * sc, 512 - 128 * sc
                            geo.append((qoff, N))
                            kc0 = wsrc * 512 + sc * 128
                            ksrc = kf2 if typ == "prev" else kf
                            vT = C.op("tensor", lambda e, c=c, hp=hp, kc0=kc0, sc=sc,
                                      qoff=qoff, N=N, t0=t0, ksrc=ksrc: e.matmul(
                                psum[:, sc * 512: sc * 512 + N],
                                ksrc[hp * 64:(hp + 1) * 64,
                                     c * T + kc0: c * T + kc0 + 128],
                                qf[hp * 64:(hp + 1) * 64,
                                   c * T + t0 + qoff: c * T + t0 + qoff + N],
                                start=True, stop=True),
                                waits=([("sS", C.cnt["sS"]), ("sV", C.cnt["sV"])]
                                       if (first and sc == 0) else ()), inc="sT")
                            bcol = (h % 2) * 16 + (h // 2) * 4 + w
                            vS = C.op("scalar", lambda e, sc=sc, N=N, bcol=bcol:
                                      e.activation(
                                exps[:, sc * 512: sc * 512 + N],
                                psum[:, sc * 512: sc * 512 + N], AF.Exp,
                                bias=bM[:, bcol:bcol + 1], scale=1.0),
                                waits=[("sT", vT)], inc="sS")
                            vSs.append(vS)
                        vMs = []
                        for sc in range(4):
                            qoff, N = geo[sc]
                            dcol = sc * 512 + (N - 128 if typ == "prev" else 0)
                            msl = masks[:, 128:256] if typ == "prev" else masks[:, 0:128]
                            vM = C.op("vector", lambda e, dcol=dcol, msl=msl:
                                      e.tensor_tensor(
                                exps[:, dcol:dcol + 128], exps[:, dcol:dcol + 128],
                                msl, OP.mult), waits=[("sS", vSs[sc])], inc="sV")
                            vMs.append(vM)
                        if (dbgE is not None and lay == 0 and seq == 0
                                and w == 0 and h == 0):
                            C.dma("sync", dbgE.ap()[:, :], exps[:, 0:2048],
                                  waits=[("sV", vMs[-1])], inc="dO")
                            C.barrier()
                        for sc in range(4):
                            qoff, N = geo[sc]
                            ktile = wsrc * 4 + sc
                            st = first and sc == 0
                            vT = C.op("tensor", lambda e, kt=ktile, h=h, pav=pav,
                                      sc=sc, qoff=qoff, N=N, st=st: e.matmul(
                                pav[:, qoff:qoff + N],
                                ivsb[:, kt * 576 + h * 72: kt * 576 + h * 72 + 72],
                                exps[:, sc * 512: sc * 512 + N],
                                start=st, stop=False),
                                waits=[("sV", vMs[sc])] +
                                      ([("sV", avEv[avb]), ("dO", C.cnt["dO"])]
                                       if st else []), inc="sT")
                        first = False
                    vT = C.op("tensor", lambda e: e.nop(), inc="sT")
                    if hp == 0:
                        vV = C.op("vector", lambda e, pav=pav, c=c: e.tensor_copy(
                            oat[0:64, c * 512:(c + 1) * 512], pav[0:64, :]),
                            waits=[("sT", vT)], inc="sV")
                    else:
                        sg = stage64[:, (c % 2) * 512:(c % 2) * 512 + 512]
                        vV = C.op("vector", lambda e, pav=pav, sg=sg: e.tensor_copy(
                            sg, pav[0:64, :]),
                            waits=[("sT", vT), ("dO", stageO[c % 2])], inc="sV")
                        stageO[c % 2] = C.dma(
                            "sync", oat[64:128, c * 512:(c + 1) * 512], sg,
                            waits=[("sV", vV)], inc="dO")
                    vV = C.op("vector", lambda e, pav=pav: e.tensor_tensor(
                        giw[64:72, :], giw[64:72, :], pav[64:72, :], OP.add),
                        waits=[("sG", vFz)], inc="sV")
                    avEv[avb] = vV
                if dbgR is not None and lay == 0 and seq == 0 and w == 0:
                    vDD = C.op("vector", lambda e: e.tensor_copy(
                        nb_mu[0:72, 0:512], giw[0:72, :]),
                        waits=[("sV", C.cnt["sV"])], inc="sV")
                    C.dma("sync", dbgR.ap()[0:72, 0:512], nb_mu[0:72, 0:512],
                          waits=[("sV", vDD)], inc="dO")
                    C.dma("sync", dbgR.ap()[:, 512:544], bM[:, :], inc="dO")
                    C.barrier()
                C.op("vector", lambda e: e.tensor_scalar(
                    giw[64:72, :], giw[64:72, :], 1e-37, None, OP.max), inc="sV")
                vV = C.op("vector", lambda e: e.reciprocal(
                    giw[64:72, :], giw[64:72, :]), inc="sV")
                vSc = 0
                for c2 in range(4):
                    bank = 6 + c2 % 2
                    pg = pbank(bank)
                    vT = C.op("tensor", lambda e, c2=c2, pg=pg: e.matmul(
                        pg, ebi_s[64:72, c2 * 128:(c2 + 1) * 128], giw[64:72, :],
                        start=True, stop=True),
                        waits=[("sV", vV), ("sV", opEv[c2 % 2]),
                               ("sV", vGb[c2 % 2])], inc="sT")
                    vSc = C.op("vector", lambda e, c2=c2, pg=pg: e.tensor_tensor(
                        oat[:, c2 * 512:(c2 + 1) * 512],
                        oat[:, c2 * 512:(c2 + 1) * 512], pg, OP.mult),
                        waits=[("sT", vT), ("dO", C.cnt["dO"])], inc="sV")
                    vGb[c2 % 2] = vSc
                vLz, _ = zload(seq, w, halo=False)
                for fo in range(NC):
                    b2_ = fo % 2
                    po = pbank(6 + b2_)
                    for k in range(4):
                        vT = C.op("tensor", lambda e, fo=fo, k=k, po=po, st=(k == 0):
                                  e.matmul(
                            po, iwo_sb[:, k * D + fo * 128: k * D + (fo + 1) * 128],
                            oat[:, k * 512:(k + 1) * 512], start=st, stop=(k == 3)),
                            waits=[("sV", vSc), ("sV", opEv[b2_])] if k == 0 else (),
                            inc="sT")
                    vE = C.op("vector", lambda e, po=po, fo=fo: e.tensor_tensor(
                        zt[:, fo * 513: fo * 513 + 512], po,
                        zt[:, fo * 513: fo * 513 + 512], OP.add),
                        waits=[("sT", vT), ("dL", vLz)], inc="sV")
                    opEv[b2_] = vE
                    C.dma("sync", zh.ap()[fo * 128:(fo + 1) * 128,
                                          seq * T + t0: seq * T + t0 + 512],
                          zt[:, fo * 513: fo * 513 + 512],
                          waits=[("sV", vE)], inc="dO")
                C.barrier()
        C.barrier()
        if qkh is not None and lay == 0:
            for i, buf in enumerate((qf, kf, kf2)):
                C.dma("sync", mkap(qkh, i * 128 * 4 * T, [[4 * T, 128], [1, 4 * T]]),
                      buf[:, :], inc="dO")
            C.barrier()
        s2a.close()
        zsnap_take(1 if lay == 0 else 3)

        s2b = ExitStack()
        w1_sb = salloc(s2b, "w1_sb", (128, NC * 2 * HIDP), BF16)
        w2_sb = salloc(s2b, "w2_sb", (128, 22 * D), BF16)
        gel = salloc(s2b, "gel", (128, 22 * 512), BF16)
        lno = salloc(s2b, "lno", (128, NC * 512), BF16)
        prep = salloc(s2b, "prepb", (128, 512), F32)
        x1t2 = salloc(s2b, "x1t2", (128, 2 * 512), F32)
        hbuf = salloc(s2b, "hbuf", (128, 32), F32)
        C.barrier()
        for k in range(NC):
            C.dma("gpsimd", w1_sb[:, k * 2 * HIDP:(k + 1) * 2 * HIDP],
                  mkap(w1p, lay * D * 2 * HIDP + k * 128 * 2 * HIDP,
                       [[2 * HIDP, 128], [1, 2 * HIDP]]),
                  waits=[("dO", C.cnt["dO"]), ("dH", C.cnt["dH"]),
                         ("dL", C.cnt["dL"])] if k == 0 else (), inc="dW")
        for k in range(22):
            C.dma("gpsimd", w2_sb[:, k * D:(k + 1) * D],
                  mkap(w2p, lay * HIDP * D + k * 128 * D, [[D, 128], [1, D]]),
                  inc="dW")
        C.dbarrier()

        for seq in range(SPC):
            opEv = {0: 0, 1: 0}
            vHb = 0
            for c in range(NC):
                vHb = C.op("sync", lambda e, c=c: e.dma_start(
                    hbuf[:, c * 4 + 1: c * 4 + 4],
                    mkap(zh, (c * 128) * SPC * T + seq * T + 511,
                         [[SPC * T, 128], [512, 3]]),
                    allow_slow_non_contiguous=True),
                    waits=[("dO", C.cnt["dO"]), ("sG", C.cnt["sG"])]
                    if c == 0 else (), inc="dL", inc_amt=16)
            for ti in range(NT):
                t0 = ti * 512
                vL, vG0 = zload_pre(seq, ti, hbuf, vHb)

                def h2view(c):
                    return (zt[:, c * 513 + 1: c * 513 + 513] if c < 4 else
                            zt[:, c * 513 + 0: c * 513 + 512])
                C.pend["vector"].append(lambda e, v=vL: e.wait_ge(sems["dL"], v))
                C.pend["scalar"].append(lambda e, v=vL: e.wait_ge(sems["dL"], v))
                C.pend["vector"].append(lambda e, v=vG0: e.wait_ge(sems["sG"], v))
                C.pend["scalar"].append(lambda e, v=vG0: e.wait_ge(sems["sG"], v))
                vT = ln_stats(h2view, 0, 1)
                vH = ln_rows_bcast(vT, 0, 1)
                vLn = 0
                for c in range(NC):
                    xs = h2view(c)
                    vG = C.op("gpsimd", lambda e, xs=xs: e.tensor_tensor(
                        prep[:, :], xs, nb_mu[:, :], OP.add),
                        waits=[("dH", vH), ("sV", vLn)], inc="sG")
                    vV = C.op("vector", lambda e, c=c: e.scalar_tensor_tensor(
                        prep[:, :], prep[:, :],
                        lng_s[:, lay * 8 + c: lay * 8 + c + 1],
                        nb_inv[:, :], OP.mult, OP.mult), waits=[("sG", vG)],
                        inc="sV")
                    vLn = C.op("vector", lambda e, c=c: e.tensor_scalar(
                        lno[:, c * 512:(c + 1) * 512], prep[:, :],
                        lnb_s[:, lay * 8 + c: lay * 8 + c + 1], None, OP.add),
                        inc="sV")
                lastE = {0: ("sV", 0), 1: ("sV", 0)}
                for f in range(44):
                    fo = f + 22 if f < 22 else f - 22
                    bank = f % 2
                    po = pbank(bank)
                    lsem, lcnt = lastE[bank]
                    for k in range(NC):
                        vT = C.op("tensor", lambda e, fo=fo, k=k, po=po, st=(k == 0):
                                  e.matmul(
                            po, w1_sb[:, k * 2 * HIDP + fo * 128:
                                      k * 2 * HIDP + (fo + 1) * 128],
                            lno[:, k * 512:(k + 1) * 512], start=st,
                            stop=(k == NC - 1)),
                            waits=([("sV", vLn), (lsem, lcnt)] if k == 0 else ()),
                            inc="sT")
                    if fo >= 22:
                        gi = fo - 22
                        vE = C.op("scalar", lambda e, po=po, gi=gi: e.activation(
                            gel[:, gi * 512:(gi + 1) * 512], po, AF.Gelu,
                            bias=b1_s[:, lay * 44 + 22 + gi: lay * 44 + 22 + gi + 1],
                            scale=1.0), waits=[("sT", vT)], inc="sS")
                        lastE[bank] = ("sS", vE)
                    else:
                        ai = fo
                        vE = C.op("vector", lambda e, po=po, ai=ai:
                                  e.scalar_tensor_tensor(
                            gel[:, ai * 512:(ai + 1) * 512], po,
                            b1_s[:, lay * 44 + ai: lay * 44 + ai + 1],
                            gel[:, ai * 512:(ai + 1) * 512], OP.add, OP.mult),
                            waits=[("sT", vT), ("sS", C.cnt["sS"])], inc="sV")
                        lastE[bank] = ("sV", vE)
                vH4 = C.op("vector", lambda e: e.nop(), inc="sV")
                for fo in range(NC):
                    b2_ = fo % 2
                    po = pbank(4 + b2_)
                    for k in range(22):
                        vT = C.op("tensor", lambda e, fo=fo, k=k, po=po, st=(k == 0):
                                  e.matmul(
                            po, w2_sb[:, k * D + fo * 128: k * D + (fo + 1) * 128],
                            gel[:, k * 512:(k + 1) * 512], start=st, stop=(k == 21)),
                            waits=([("sV", vH4), ("sV", opEv[b2_])] if k == 0
                                   else ()), inc="sT")
                    zres = zt[:, fo * 513 + 1: fo * 513 + 513]
                    vE = C.op("vector", lambda e, po=po, fo=fo, zres=zres:
                              e.scalar_tensor_tensor(
                        zres, po, b2_s[:, lay * 8 + fo: lay * 8 + fo + 1],
                        zres, OP.add, OP.add),
                        waits=[("sT", vT)], inc="sV")
                    if lay == 0:
                        opEv[b2_] = vE
                        C.dma("sync", zh.ap()[fo * 128:(fo + 1) * 128,
                                              seq * T + t0: seq * T + t0 + 512],
                              zres, waits=[("sV", vE)], inc="dO")
                    else:
                        x1s = x1t2[:, b2_ * 512:(b2_ + 1) * 512]
                        vX = C.dma("sync", x1s,
                                   x1h.ap()[fo * 128:(fo + 1) * 128,
                                            seq * T + t0: seq * T + t0 + 512],
                                   waits=[("sV", opEv[b2_])], inc="dL")
                        vE = C.op("vector", lambda e, fo=fo, zres=zres, x1s=x1s:
                                  e.tensor_tensor(
                            zres, zres, x1s, OP.add),
                            waits=[("dL", vX)], inc="sV")
                        opEv[b2_] = vE
                        C.dma("sync", out_t.ap()[fo * 128:(fo + 1) * 128,
                                                 seq * T + t0: seq * T + t0 + 512],
                              zres, waits=[("sV", vE)], inc="dO")
                C.barrier()
        C.barrier()
        s2b.close()
        if lay == 0:
            zsnap_take(2)

    C.op("sync", lambda e: e.nop(), waits=[("dO", C.cnt["dO"]),
                                           ("dW", C.cnt["dW"]),
                                           ("dL", C.cnt["dL"]),
                                           ("dH", C.cnt["dH"])])
    C.flush()
    sS2.close()
    es.close()
    return nc


_SHARED = ["wqkvT", "woutT", "iwqkv", "iwo", "w1p", "w2p", "b1t", "b2t", "lngt",
           "lnbt", "qst", "kst", "qswt", "kswt", "rcq", "rsq", "rck", "rsk",
           "mcur", "mprev", "e2", "onescol", "abdt", "bsq", "ebo", "ebi"]


def kernel(**inputs):
    d = _host_prep(inputs)
    nc = build_program()
    in_maps = []
    for i in range(NCORE):
        m = {"xT": d["_x_per_core"][i]}
        for k in _SHARED:
            m[k] = d[k]
        in_maps.append(m)
    res = run_bass_kernel_spmd(nc, in_maps, core_ids=list(range(NCORE)))
    outs = []
    for i in range(NCORE):
        o = np.asarray(res.results[i]["out"])
        o = o.reshape(D, SPC, T).transpose(1, 2, 0)
        outs.append(o)
    return np.ascontiguousarray(np.concatenate(outs, axis=0)).astype(np.float32)
